# version 14
# baseline (speedup 1.0000x reference)
import contextlib

import numpy as np

import concourse.bass as bass
import concourse.mybir as mybir
from concourse.bass_utils import run_bass_kernel_spmd

F32 = mybir.dt.float32
BF16 = mybir.dt.bfloat16
AF = mybir.ActivationFunctionType
ALU = mybir.AluOpType

D = 1024
DC = 8
NSEQ = 8192
NQ = 4224
NKT = 64
SUPERS = [12, 12, 12, 12, 12, 4]
CHUNK_T = [6, 6, 6, 6, 6, 3]
RING = 15
QG = 256
EPS = 1e-5
EXPB = -10.0


class Plan:

    def __init__(self, nc, stack, sem_names):
        self.nc = nc
        self.ops = []
        self.cnt = {}
        self.sem = {}
        for n in sem_names:
            self.sem[n] = stack.enter_context(nc.semaphore(n))
            self.cnt[n] = 0

    def add(self, eng, emit, waits=(), inc=None, amt=1):
        ws = []
        for sname, v in waits:
            assert v <= self.cnt[sname], (sname, v, self.cnt[sname])
            if v > 0:
                ws.append((sname, v))
        if inc is not None:
            self.cnt[inc] += amt
        self.ops.append((eng, emit, tuple(ws), inc, amt))
        return self.cnt.get(inc)

    def val(self, name):
        return self.cnt[name]

    def emit_into(self, blk):
        by_eng = {}
        for op in self.ops:
            by_eng.setdefault(op[0], []).append(op)
        self.ops = []
        for engname in ("sync", "tensor", "scalar", "vector", "gpsimd"):
            ops = by_eng.get(engname)
            if not ops:
                continue

            def body(eng, _ops=ops, _p=self):
                for _, emit, ws, inc, amt in _ops:
                    for sname, v in ws:
                        eng.wait_ge(_p.sem[sname], v)
                    last = emit(eng)
                    if inc is not None:
                        last.then_inc(_p.sem[inc], amt)

            getattr(blk, engname)(body)

    def block(self):
        with self.nc.Block() as blk:
            self.emit_into(blk)



def _tp(e, out, in_, ident):
    e.nop(cycle_cnt=32)
    r = e.transpose(out, in_, ident)
    e.nop(cycle_cnt=32)
    return r

def build(dbg=False, phases="AFE", ngf=16, ngq=9, nwt=4, skip_bias=False, skip_wqkv=False, epi_tiles=99, epi_skip=(), la=True):
    nc = bass.Bass("TRN2", target_bir_lowering=False)
    SCR = "ExternalOutput" if dbg else "Internal"

    XQ = nc.dram_tensor("XQ", [NQ, D], F32, kind="ExternalInput")
    XF = nc.dram_tensor("XF", [NSEQ, D], F32, kind="ExternalInput")
    WQKV = nc.dram_tensor("WQKV", [D, 3 * D], F32, kind="ExternalInput")
    WOUT = nc.dram_tensor("WOUT", [D, D], F32, kind="ExternalInput")
    OUTB = nc.dram_tensor("OUTB", [D], F32, kind="ExternalInput")
    WR = nc.dram_tensor("WR", [D, D], F32, kind="ExternalInput")
    WK = nc.dram_tensor("WK", [D, D], F32, kind="ExternalInput")
    WV = nc.dram_tensor("WV", [D, D], F32, kind="ExternalInput")
    MUR = nc.dram_tensor("MUR", [D], F32, kind="ExternalInput")
    MUK = nc.dram_tensor("MUK", [D], F32, kind="ExternalInput")
    EYE = nc.dram_tensor("EYE", [128, 128], F32, kind="ExternalInput")
    Y = nc.dram_tensor("Y", [NQ, D], F32, kind="ExternalOutput")
    if dbg:
        DBGW = nc.dram_tensor("DBGW", [4, 128, DC, D], BF16, kind="ExternalOutput")
        DBGC = nc.dram_tensor("DBGC", [6, 128, 8], F32, kind="ExternalOutput")
        DBGN = nc.dram_tensor("DBGN", [NSEQ, D], BF16, kind="ExternalOutput")
        DBGO = nc.dram_tensor("DBGO", [6, 128, DC, 768], F32, kind="ExternalOutput")
        DBGP = nc.dram_tensor("DBGP", [6, 128, 768], F32, kind="ExternalOutput")
        DBGL = nc.dram_tensor("DBGL", [6, 768], F32, kind="ExternalOutput")
        DBGA = nc.dram_tensor("DBGA", [NQ, D], F32, kind="ExternalOutput")
        DBGX1 = nc.dram_tensor("DBGX1", [NQ, D], F32, kind="ExternalOutput")

    KS = nc.dram_tensor("KS", [NKT, 128, DC, 128], BF16, kind=SCR)
    VS = nc.dram_tensor("VS", [NKT, 128, D], BF16, kind=SCR)
    QS = nc.dram_tensor("QS", [DC, 128, NQ], BF16, kind=SCR)
    LS = nc.dram_tensor("LS", [768], F32, kind="Internal")

    sems = (["T", "V", "S", "qS", "qQ", "qWk", "qWv", "qWq", "qWw", "qOut",
             "fS", "fE", "fP", "fO", "fF", "qL", "qWt", "qQ0", "qQ1", "qQ2", "G"] + [f"sl{i}" for i in range(RING)])

    stack = contextlib.ExitStack()
    with stack:
        p = Plan(nc, stack, sems)

        def sb(name, shape, dt):
            return stack.enter_context(nc.sbuf_tensor(name, shape, dt))

        woutT = sb("woutT", [128, DC, D], BF16)
        wrT = sb("wrT", [128, DC, D], BF16)
        wkT = sb("wkT", [128, DC, D], BF16)
        wvT = sb("wvT", [128, DC, D], BF16)
        biasb = sb("biasb", [128, D], BF16)
        murf = sb("murf", [128, DC], F32)
        mukf = sb("mukf", [128, DC], F32)
        mure = sb("mure", [128, DC, 128], BF16)
        muke = sb("muke", [128, DC, 128], BF16)
        eyef = sb("eyef", [128, 128], F32)
        eyeb = sb("eyeb", [128, 128], BF16)
        ones128 = sb("ones128", [128, 128], BF16)
        cexp = sb("cexp", [128, 1], F32)
        cmh = sb("cmh", [128, 1], F32)
        ceps = sb("ceps", [128, 1], F32)

        with contextlib.ExitStack() as ascope:
            def asb(name, shape, dt):
                return ascope.enter_context(nc.sbuf_tensor(name, shape, dt))

            wqkv_b = asb("wqkv_b", [128, DC, 3 * D], BF16)
            onesrow = asb("onesrow", [128, 128], BF16)
            obmat = asb("obmat", [128, D], BF16)
            wtmp = asb("wtmp", [128, 2, D], BF16)
            xst = asb("xst", [128, 4, D], F32)
            lnb = asb("lnb", [128, 4, D], BF16)
            xT = asb("xT", [128, 2, DC, 512], BF16)
            kst = asb("kst", [128, 2, 512], BF16)
            vst = asb("vst", [128, 2, D], BF16)
            qst = asb("qst", [128, 2, 512], BF16)
            bnst = asb("bnst", [128, 12], F32)
            mvA = asb("mvA", [128, 4, 2], F32)
            rstdA = asb("rstdA", [128, 4, 2], F32)
            pTx = ascope.enter_context(nc.psum_tensor("pTx", [128, 2, DC, 128], BF16))
            pK = ascope.enter_context(nc.psum_tensor("pK", [128, 2, 512], F32))
            pV = ascope.enter_context(nc.psum_tensor("pV", [128, 2, D], F32))

            for wu in range(256):
                p.add("tensor", lambda e: e.matmul(
                    pV[:, 0, 0:512], wqkv_b[:, 0, 0:128], wqkv_b[:, 0, 0:512],
                    start=True, stop=True))
            for wu in range(16):
                p.add("tensor", lambda e: _tp(e, pTx[:, 0, wu % 8, :],
                                              wqkv_b[:, 0, 0:128], wqkv_b[:, 0, 128:256]))
            p.add("tensor", lambda e: e.nop(), inc="T")

            p.add("vector", lambda e: e.memset(cmh[:, :], -0.5))
            p.add("vector", lambda e: e.memset(cexp[:, :], EXPB))
            p.add("vector", lambda e: e.memset(ceps[:, :], EPS))
            p.add("vector", lambda e: e.memset(onesrow[:, :], 0.0))
            p.add("vector", lambda e: e.memset(onesrow[0:1, :], 1.0))
            p.add("vector", lambda e: e.memset(ones128[:, :], 1.0))
            p.add("vector", lambda e: e.memset(obmat[:, :], 0.0), inc="V")
            v_const = p.val("V")

            p.add("sync", lambda e: e.dma_start(eyef[:, :], EYE[:, :]), inc="qS", amt=16)
            p.add("sync", lambda e: e.dma_start(
                murf[:, :], MUR.ap().rearrange("(c p) -> p c", p=128),
                allow_slow_non_contiguous=True), inc="qS", amt=16)
            p.add("sync", lambda e: e.dma_start(
                mukf[:, :], MUK.ap().rearrange("(c p) -> p c", p=128),
                allow_slow_non_contiguous=True), inc="qS", amt=16)
            q_small = p.val("qS")
            p.add("gpsimd", lambda e: e.dma_start(
                obmat[0:1, :], OUTB.ap().rearrange("(a d) -> a d", a=1)),
                waits=[("V", v_const)], inc="qWw", amt=16)

            for d in range(DC if not skip_wqkv else 0):
                p.add("gpsimd", lambda e, d=d: e.dma_start(
                    wqkv_b[:, d, :], WQKV[d * 128:(d + 1) * 128, :]),
                    inc="qWw", amt=16)
            q_wqkv = p.val("qWw")

            p.add("vector", lambda e: e.tensor_copy(eyeb[:, :], eyef[:, :]),
                  waits=[("qS", q_small)], inc="V")
            v_eyeb = p.val("V")
            if not skip_bias:
                p.add("tensor", lambda e: e.matmul(
                    pV[:, 0, 0:512], onesrow[:, :], obmat[:, 0:512], start=True, stop=True),
                    waits=[("qS", q_small), ("qWw", p.val("qWw")), ("V", v_const)])
                p.add("tensor", lambda e: e.matmul(
                    pK[:, 0, 0:512], onesrow[:, :], obmat[:, 512:1024],
                    start=True, stop=True), inc="T")
                t_bias = p.val("T")
                p.add("vector", lambda e: e.tensor_copy(biasb[:, 0:512], pV[:, 0, 0:512]),
                      waits=[("T", t_bias)])
                p.add("vector", lambda e: e.tensor_copy(
                    biasb[:, 512:1024], pK[:, 0, 0:512]), inc="V")

            for c in range(DC):
                p.add("vector", lambda e, c=c: e.tensor_scalar(
                    mure[:, c, :], ones128[:, :], murf[:, c:c + 1], None,
                    op0=ALU.mult), waits=([("qS", q_small)] if c == 0 else []))
            for c in range(DC):
                p.add("vector", lambda e, c=c: e.tensor_scalar(
                    muke[:, c, :], ones128[:, :], mukf[:, c:c + 1], None,
                    op0=ALU.mult), inc=("V" if c == DC - 1 else None))

            wt_ev = {}
            for wi, (Wd, Wt) in enumerate(
                    [(WOUT, woutT), (WR, wrT), (WK, wkT), (WV, wvT)][:nwt]):
                for ch in range(DC):
                    n = wi * DC + ch
                    par = n % 2
                    lws = [("V", wt_ev[n - 2])] if n - 2 in wt_ev else []
                    p.add("gpsimd", lambda e, Wd=Wd, ch=ch, par=par: e.dma_start(
                        wtmp[:, par, :], Wd[ch * 128:(ch + 1) * 128, :]),
                        waits=lws, inc="qWt", amt=16)
                    qv = p.val("qWt")
                    tws = [("qWt", qv), ("V", v_eyeb)]
                    if n - 2 in wt_ev:
                        tws.append(("V", wt_ev[n - 2]))
                    for bi in range(DC):
                        p.add("tensor", lambda e, par=par, bi=bi: _tp(e, 
                            pTx[:, par, bi, :],
                            wtmp[:, par, bi * 128:(bi + 1) * 128], eyeb[:, :]),
                            waits=(tws if bi == 0 else []),
                            inc=("T" if bi == DC - 1 else None))
                    tv = p.val("T")
                    p.add("vector", lambda e, Wt=Wt, par=par, ch=ch: e.tensor_copy(
                        Wt[:, :, ch * 128:(ch + 1) * 128], pTx[:, par, :, :]),
                        waits=[("T", tv)], inc="V")
                    wt_ev[n] = p.val("V")

            last_ln = {}
            last_tp = {}
            last_ev = {}
            last_mm_grp = {}
            kev = {}
            vev = {}
            qev = {}
            nwk = [0]
            nwv = [0]
            nwq = [0]
            tile_seq = [0]
            grp_seq = [0]
            KS_p = KS.ap().rearrange("t p c k -> p t c k")

            pend_mm = []

            def emit_group(SRC, t0, ntl, kinds):
                g = grp_seq[0]
                grp_seq[0] += 1
                gpar = g % 2
                tw = ntl * 128
                svs = {}

                def stage_a(j, ts, par, tt):
                    lws = [("V", last_ln[ts - 4])] if ts - 4 in last_ln else []
                    p.add("sync", lambda e, SRC=SRC, tt=tt, par=par: e.dma_start(
                        xst[:, par, :], SRC[tt * 128:(tt + 1) * 128, :]),
                        waits=lws, inc="qS", amt=16)
                    qv = p.val("qS")
                    p.add("vector", lambda e, par=par: e.bn_stats(
                        bnst[:, 0:6], xst[:, par, 0:512]), waits=[("qS", qv)])
                    p.add("vector", lambda e, par=par: e.bn_stats(
                        bnst[:, 6:12], xst[:, par, 512:1024]), inc="V")
                    _bv = p.val("V")
                    p.add("vector", lambda e, par=par: e.bn_aggr(
                        mvA[:, par, :], bnst[:, :]), waits=[("V", _bv)], inc="V")
                    vv = p.val("V")
                    p.add("scalar", lambda e, par=par: e.activation(
                        rstdA[:, par, 0:1], mvA[:, par, 1:2], AF.Ln,
                        bias=ceps[:, 0:1]), waits=[("V", vv)], inc="S")
                    _sv = p.val("S")
                    p.add("scalar", lambda e, par=par: e.activation(
                        rstdA[:, par, 1:2], rstdA[:, par, 0:1], AF.Exp,
                        scale=-0.5), waits=[("S", _sv)], inc="S")
                    svs[ts] = p.val("S")

                def norm(j, ts, par):
                    nws = [("S", svs[ts])]
                    if ts - 4 in last_tp:
                        nws.append(("T", last_tp[ts - 4]))
                    p.add("vector", lambda e, par=par: e.tensor_scalar(
                        lnb[:, par, :], xst[:, par, :], mvA[:, par, 0:1],
                        rstdA[:, par, 1:2],
                        op0=ALU.subtract, op1=ALU.mult), waits=nws, inc="V")
                    last_ln[ts] = p.val("V")
                    if dbg and SRC is XF:
                        p.add("gpsimd", lambda e, par=par, ts=ts: e.dma_start(
                            DBGN[ts * 128:(ts + 1) * 128, :], lnb[:, par, :]),
                            waits=[("V", last_ln[ts])], inc="qWw", amt=16)

                def tp_unit(j, ts, par):
                    pp = ts % 2
                    tws = [("V", last_ln[ts])]
                    if ts - 2 in last_ev:
                        tws.append(("V", last_ev[ts - 2]))
                    for c in range(DC):
                        p.add("tensor", lambda e, par=par, pp=pp, c=c: _tp(e,
                            pTx[:, pp, c, :], lnb[:, par, c * 128:(c + 1) * 128],
                            eyeb[:, :]), waits=(tws if c == 0 else []),
                            inc=("T" if c == DC - 1 else None))
                    last_tp[ts] = p.val("T")
                    ews = [("T", last_tp[ts])]
                    if g - 2 in last_mm_grp:
                        ews.append(("T", last_mm_grp[g - 2]))
                    p.add("vector", lambda e, pp=pp, gpar=gpar, j=j: e.tensor_copy(
                        xT[:, gpar, :, j * 128:(j + 1) * 128], pTx[:, pp, :, :]),
                        waits=ews, inc="V")
                    last_ev[ts] = p.val("V")

                tss = []
                for j in range(ntl):
                    ts = tile_seq[0]
                    tile_seq[0] += 1
                    tss.append((j, ts, ts % 4))
                    stage_a(j, ts, ts % 4, t0 + j)
                for t in tss:
                    norm(*t)
                tp_units = [(lambda t=t: tp_unit(*t)) for t in tss]

                if pend_mm:
                    pgid, punits = pend_mm.pop()
                    mi = ti = 0
                    while mi < len(punits) or ti < len(tp_units):
                        for _ in range(2):
                            if mi < len(punits):
                                punits[mi]()
                                mi += 1
                        if ti < len(tp_units):
                            tp_units[ti]()
                            ti += 1
                    last_mm_grp[pgid] = p.val("T")
                else:
                    for f in tp_units:
                        f()

                def xtr():
                    return [("V", last_ev[tss[-1][1]]), ("qWw", q_wqkv)]

                mm_units = []
                if "K" in kinds:
                    for c in range(DC):
                        def mk_k(c=c):
                            ws = xtr() if c == 0 else []
                            if nwk[0] - 2 in kev:
                                ws.append(("S", kev[nwk[0] - 2]))
                            for d in range(DC):
                                p.add("tensor",
                                      lambda e, c=c, d=d, gpar=gpar, tw=tw: e.matmul(
                                          pK[:, c % 2, 0:tw],
                                          wqkv_b[:, d, D + c * 128:D + (c + 1) * 128],
                                          xT[:, gpar, d, 0:tw],
                                          start=(d == 0), stop=(d == DC - 1)),
                                      waits=(ws if d == 0 else []),
                                      inc=("T" if d == DC - 1 else None))
                            tv = p.val("T")
                            p.add("scalar", lambda e, c=c, tw=tw: e.activation(
                                kst[:, c % 2, 0:tw], pK[:, c % 2, 0:tw], AF.Copy),
                                waits=[("T", tv), ("qWk", 16 * max(0, nwk[0] - 1))],
                                inc="S")
                            kev[nwk[0]] = p.val("S")
                            sv = p.val("S")
                            p.add("gpsimd",
                                  lambda e, c=c, t0=t0, ntl=ntl: e.dma_start(
                                      KS_p[:, t0:t0 + ntl, c, :],
                                      kst[:, c % 2, 0:ntl * 128].rearrange(
                                          "p (t k) -> p t k", k=128)),
                                  waits=[("S", sv)], inc="qWk", amt=16)
                            nwk[0] += 1
                        mm_units.append(mk_k)
                if "V" in kinds:
                    for j in range(ntl):
                        def mk_v(j=j):
                            ws = xtr() if j == 0 else []
                            if nwv[0] - 2 in vev:
                                ws.append(("S", vev[nwv[0] - 2]))
                            for d in range(DC):
                                for hh in range(2):
                                    p.add("tensor",
                                          lambda e, j=j, d=d, hh=hh, gpar=gpar: e.matmul(
                                              pV[:, j % 2, hh * 512:(hh + 1) * 512],
                                              xT[:, gpar, d, j * 128:(j + 1) * 128],
                                              wqkv_b[:, d,
                                                     2 * D + hh * 512:2 * D + (hh + 1) * 512],
                                              start=(d == 0), stop=(d == DC - 1)),
                                          waits=(ws if (d == 0 and hh == 0) else []),
                                          inc=("T" if (d == DC - 1 and hh == 1) else None))
                            tv = p.val("T")
                            p.add("scalar", lambda e, j=j: e.activation(
                                vst[:, j % 2, :], pV[:, j % 2, :], AF.Copy),
                                waits=[("T", tv), ("qWv", 16 * max(0, nwv[0] - 1))],
                                inc="S")
                            vev[nwv[0]] = p.val("S")
                            sv = p.val("S")
                            p.add("gpsimd", lambda e, j=j, t0=t0: e.dma_start(
                                VS[t0 + j, :, :], vst[:, j % 2, :]),
                                waits=[("S", sv)], inc="qWv", amt=16)
                            nwv[0] += 1
                        mm_units.append(mk_v)
                if "Q" in kinds:
                    for c in range(DC):
                        def mk_q(c=c):
                            ws = xtr() if c == 0 else []
                            if nwq[0] - 2 in qev:
                                ws.append(("S", qev[nwq[0] - 2]))
                            for d in range(DC):
                                p.add("tensor",
                                      lambda e, c=c, d=d, gpar=gpar, tw=tw: e.matmul(
                                          pK[:, c % 2, 0:tw],
                                          wqkv_b[:, d, c * 128:(c + 1) * 128],
                                          xT[:, gpar, d, 0:tw],
                                          start=(d == 0), stop=(d == DC - 1)),
                                      waits=(ws if d == 0 else []),
                                      inc=("T" if d == DC - 1 else None))
                            tv = p.val("T")
                            p.add("scalar", lambda e, c=c, tw=tw: e.activation(
                                qst[:, c % 2, 0:tw], pK[:, c % 2, 0:tw], AF.Copy,
                                scale=1.0 / 32.0),
                                waits=[("T", tv), ("qWq", 16 * max(0, nwq[0] - 1))],
                                inc="S")
                            qev[nwq[0]] = p.val("S")
                            sv = p.val("S")
                            p.add("gpsimd", lambda e, c=c, t0=t0, tw=tw: e.dma_start(
                                QS[c, :, t0 * 128:t0 * 128 + tw], qst[:, c % 2, 0:tw]),
                                waits=[("S", sv)], inc="qWq", amt=16)
                            nwq[0] += 1
                        mm_units.append(mk_q)
                pend_mm.append((g, mm_units))

            for g in range(min(16, ngf)):
                emit_group(XF, 4 * g, 4, ("K", "V"))
            for g in range(min(8, max(0, ngq - 1))):
                emit_group(XQ, 4 * g, 4, ("Q",))
            if ngq >= 9:
                emit_group(XQ, 32, 1, ("Q",))
            while pend_mm:
                pgid, punits = pend_mm.pop()
                for f in punits:
                    f()
                last_mm_grp[pgid] = p.val("T")


            p.block()

        scr_k = 16 * nwk[0]
        scr_v = 16 * nwv[0]
        scr_q = 16 * nwq[0]

        if phases == "A":
            p.add("sync", lambda e: e.nop(), waits=[("qWk", scr_k)])
            p.add("scalar", lambda e: e.nop(), waits=[("qWv", scr_v), ("qWq", scr_q)])
            p.add("gpsimd", lambda e: e.dma_start(Y[0:128, :], biasb[:, :]),
                  inc="qOut", amt=16)
            p.add("sync", lambda e: e.nop(), waits=[("qOut", 16)])
            p.block()
            return nc

        Oacc = sb("Oacc", [128, DC, 768], F32)
        Pacc = sb("Pacc", [128, 768], F32)
        PaccB = sb("PaccB", [128, 768], BF16)
        h2f = sb("h2f", [128, DC, 129], BF16)
        invl_bc = sb("invl_bc", [128, 768], F32)
        invl_cols = sb("invl_cols", [128, 8], F32)
        Kring = sb("Kring", [128, RING, DC, 128], BF16)
        Vring = sb("Vring", [128, RING, D], BF16)
        QS_p = QS.ap().rearrange("c p t -> p c t")

        kload_n = [0]
        slot_round = [0] * RING
        slot_wait = {}
        super_fO = {}
        q_wait = {}
        n_exp = [0]
        out_n = [0]
        EPV = {}

        def super_of(kt):
            acc = 0
            for si, sz in enumerate(SUPERS):
                acc += sz
                if kt < acc:
                    return si
            raise AssertionError(kt)

        def kv_guard(gkt):
            prev = gkt - RING
            if prev < 0:
                return 0
            return super_fO[(prev // NKT, super_of(prev % NKT))]

        def pump_loads(ci, upto):
            while kload_n[0] < ci * NKT + min(upto, NKT):
                gkt = kload_n[0]
                kt = gkt % NKT
                slot = gkt % RING
                g = kv_guard(gkt)
                kq = "sync" if gkt % 2 == 0 else "gpsimd"
                p.add(kq, lambda e, kt=kt, slot=slot: e.dma_start(
                    Kring[:, slot, :, :], KS[kt, :, :, :]),
                    waits=[("fO", g)], inc=f"sl{slot}", amt=16)
                p.add("scalar", lambda e, kt=kt, slot=slot: e.dma_start(
                    Vring[:, slot, :], VS[kt, :, :]),
                    waits=[("fO", g)], inc=f"sl{slot}", amt=16)
                slot_round[slot] += 1
                slot_wait[(ci, kt)] = 32 * slot_round[slot]
                kload_n[0] += 1

        for ci in range(6):
            T0 = sum(CHUNK_T[:ci])
            nt = CHUNK_T[ci]
            qws = []
            rem = nt * 128
            while rem > 0:
                qws.append(min(QG, rem))
                rem -= qws[-1]

            blkg = nc.Block()
            blk = blkg.__enter__()
            with contextlib.ExitStack() as fscope:
                Qg = fscope.enter_context(
                    nc.sbuf_tensor(f"Qg{ci}", [128, 3, DC, QG], BF16))
                Ptile = fscope.enter_context(
                    nc.sbuf_tensor(f"Ptile{ci}", [128, max(SUPERS), QG], BF16))
                pS = fscope.enter_context(nc.psum_tensor(f"pS{ci}", [128, 2, 512], F32))
                pO = fscope.enter_context(nc.psum_tensor(f"pO{ci}", [128, 4, 512], F32))

                if ci == 0:
                    if dbg:
                        for wi, Wt in enumerate([woutT, wrT, wkT, wvT]):
                            p.add("gpsimd", lambda e, wi=wi, Wt=Wt: e.dma_start(
                                DBGW[wi, :, :, :], Wt[:, :, :]), inc="qL", amt=16)
                    allscr = [("qWk", scr_k), ("qWv", scr_v), ("qWq", scr_q)]
                    p.add("sync", lambda e: e.nop(), waits=allscr)
                    p.add("scalar", lambda e: e.nop(), waits=allscr)
                    p.add("gpsimd", lambda e: e.nop(), waits=allscr)

                qoff = 0
                for gi, qw in enumerate(qws):
                    q0 = T0 * 128 + qoff
                    p.add("scalar", lambda e, gi=gi, q0=q0, qw=qw: e.dma_start(
                        Qg[:, gi, :, 0:qw], QS_p[:, :, q0:q0 + qw]),
                        inc=f"qQ{gi}", amt=16)
                    q_wait[(ci, gi)] = (f"qQ{gi}", p.val(f"qQ{gi}"))
                    qoff += qw

                kt_base = 0
                for si, ssz in enumerate(SUPERS):
                    pump_loads(ci, kt_base + RING)
                    qoff = 0
                    for gi, qw in enumerate(qws):
                        exp_first = True
                        for tl in range(ssz):
                            kt = kt_base + tl
                            ne = n_exp[0]
                            ws = [(f"sl{(ci * NKT + kt) % RING}",
                                   slot_wait[(ci, kt)])]
                            if ne >= 2:
                                ws.append(("fE", ne - 1))
                            if tl == 0:
                                ws.append(q_wait[(ci, gi)])
                            for d in range(DC):
                                p.add("tensor",
                                      lambda e, kt=kt, d=d, gi=gi, qw=qw, ne=ne,
                                      slot=(ci * NKT + kt) % RING: e.matmul(
                                          pS[:, ne % 2, 0:qw],
                                          Kring[:, slot, d, :],
                                          Qg[:, gi, d, 0:qw],
                                          start=(d == 0), stop=(d == DC - 1)),
                                      waits=(ws if d == 0 else []),
                                      inc=("fS" if d == DC - 1 else None))
                            fs = p.val("fS")
                            ews = [("fS", fs)]
                            if exp_first:
                                ews += [("fO", p.val("fO")), ("fP", p.val("fP"))]
                                exp_first = False
                            p.add("scalar", lambda e, tl=tl, qw=qw, ne=ne: e.activation(
                                Ptile[:, tl, 0:qw], pS[:, ne % 2, 0:qw], AF.Exp,
                                bias=cexp[:, 0:1]), waits=ews, inc="fE")
                            fe = p.val("fE")
                            n_exp[0] += 1
                            first_p = (si == 0 and tl == 0)
                            if first_p:
                                p.add("vector",
                                      lambda e, tl=tl, qw=qw, qoff=qoff: e.tensor_copy(
                                          Pacc[:, qoff:qoff + qw], Ptile[:, tl, 0:qw]),
                                      waits=[("fE", fe)],
                                      inc=("fP" if tl == ssz - 1 else None))
                            else:
                                p.add("vector",
                                      lambda e, tl=tl, qw=qw, qoff=qoff: e.tensor_tensor(
                                          Pacc[:, qoff:qoff + qw],
                                          Pacc[:, qoff:qoff + qw],
                                          Ptile[:, tl, 0:qw], op=ALU.add),
                                      waits=[("fE", fe)],
                                      inc=("fP" if tl == ssz - 1 else None))
                        ows = [("fE", p.val("fE")), ("fF", p.val("fF"))]
                        for tl in range(ssz):
                            kt = kt_base + tl
                            slot = (ci * NKT + kt) % RING
                            for dv in range(DC):
                                st = (tl == 0 and dv % 2 == 0)
                                p.add("tensor",
                                      lambda e, slot=slot, dv=dv, tl=tl, qw=qw,
                                      st=st, lastt=(tl == ssz - 1): e.matmul(
                                          pO[:, dv // 2,
                                             (dv % 2) * 256:(dv % 2) * 256 + qw],
                                          Vring[:, slot, dv * 128:(dv + 1) * 128],
                                          Ptile[:, tl, 0:qw],
                                          start=st, stop=lastt,
                                          skip_group_check=True),
                                      waits=(ows if (tl == 0 and dv == 0) else []),
                                      inc=("fO" if (tl == ssz - 1 and dv == DC - 1)
                                           else None))
                        fo = p.val("fO")
                        fws = [("fO", fo)]
                        for dv in range(DC):
                            if si == 0:
                                p.add("vector",
                                      lambda e, dv=dv, qw=qw, qoff=qoff: e.tensor_copy(
                                          Oacc[:, dv, qoff:qoff + qw],
                                          pO[:, dv // 2,
                                             (dv % 2) * 256:(dv % 2) * 256 + qw]),
                                      waits=(fws if dv == 0 else []),
                                      inc=("fF" if dv == DC - 1 else None))
                            else:
                                p.add("vector",
                                      lambda e, dv=dv, qw=qw, qoff=qoff: e.tensor_tensor(
                                          Oacc[:, dv, qoff:qoff + qw],
                                          Oacc[:, dv, qoff:qoff + qw],
                                          pO[:, dv // 2,
                                             (dv % 2) * 256:(dv % 2) * 256 + qw],
                                          op=ALU.add),
                                      waits=(fws if dv == 0 else []),
                                      inc=("fF" if dv == DC - 1 else None))
                        qoff += qw
                    super_fO[(ci, si)] = p.val("fO")
                    kt_base += ssz

                p.add("vector", lambda e, nt=nt: e.tensor_copy(
                    PaccB[:, 0:nt * 128], Pacc[:, 0:nt * 128]), inc="V")
                v_paccb = p.val("V")
                f_all = p.val("fF")
                p.emit_into(blk)

            if phases == "AF":
                p.add("gpsimd", lambda e, ci=ci: e.dma_start(
                    DBGO[ci, :, :, :], Oacc[:, :, :]), inc="qOut", amt=16)
                p.add("gpsimd", lambda e, ci=ci: e.dma_start(
                    DBGP[ci, :, :], Pacc[:, :]), inc="qOut", amt=16)
                out_n[0] += 2
                p.block()
                continue

            with contextlib.ExitStack() as escope:
                def esb(name, shape, dt):
                    return escope.enter_context(nc.sbuf_tensor(f"{name}{ci}", shape, dt))

                Obf = esb("Obf", [128, DC, 128], BF16)
                attn_sb = esb("attn_sb", [128, D], F32)
                h2t = esb("h2t", [128, D], BF16)
                x0t = esb("x0t", [128, 2, D], F32)
                x1t = esb("x1t", [128, 2, D], F32)
                xr = esb("xr", [128, DC, 128], BF16)
                xk = esb("xk", [128, DC, 128], BF16)
                kr_sb = esb("kr_sb", [128, 2, 512], BF16)
                k_act = esb("k_act", [128, DC, 128], BF16)
                r_sb = esb("r_sb", [128, D], BF16)
                ffn_sb = esb("ffn_sb", [128, D], F32)
                x2sb = esb("x2sb", [128, D], F32)
                bnstE = esb("bnstE", [128, 12], F32)
                gvE = esb("gvE", [128, 4], F32)
                mvE = esb("mvE", [128, 6], F32)
                rstdE = esb("rstdE", [128, 6], F32)
                peA = escope.enter_context(nc.psum_tensor(f"peA{ci}", [128, 2, 512], F32))
                peT = escope.enter_context(
                    nc.psum_tensor(f"peT{ci}", [128, 2, 8, 128], BF16))
                peK = escope.enter_context(nc.psum_tensor(f"peK{ci}", [128, 2, 512], F32))
                peV = escope.enter_context(nc.psum_tensor(f"peV{ci}", [128, 2, 512], F32))

                if ci == 0:
                    p.add("vector", lambda e: e.memset(h2f[:, :, 0:1], 0.0), inc="V")

                ncols = nt * 128
                for j, w in enumerate(range(0, ncols, 512)):
                    lw = min(512, ncols - w)
                    p.add("tensor", lambda e, j=j, w=w, lw=lw: e.matmul(
                        peA[:, j, 0:lw], ones128[:, :], PaccB[:, w:w + lw],
                        start=True, stop=True),
                        waits=([("V", v_paccb), ("fF", f_all)] if j == 0 else []),
                        inc=("T" if w + 512 >= ncols else None))
                t_l = p.val("T")
                for j, w in enumerate(range(0, ncols, 512)):
                    lw = min(512, ncols - w)
                    p.add("vector", lambda e, j=j, w=w, lw=lw: e.reciprocal(
                        invl_bc[:, w:w + lw], peA[:, j, 0:lw]),
                        waits=([("T", t_l)] if j == 0 else []),
                        inc=("V" if w + 512 >= ncols else None))
                v_invl = p.val("V")
                if ci + 1 < 6:
                    pump_loads(ci + 1, RING)
                if dbg:
                    p.add("gpsimd", lambda e, ci=ci: e.dma_start(
                        DBGO[ci, :, :, :], Oacc[:, :, :]), inc="qL", amt=16)
                if dbg:
                    p.add("gpsimd", lambda e, ci=ci, ncols=ncols: e.dma_start(
                        DBGL.ap().rearrange("c t -> c t")[ci:ci + 1, 0:ncols],
                        invl_bc[0:1, 0:ncols]),
                        waits=[("V", v_invl)], inc="qL", amt=16)

                NTE = min(nt, epi_tiles)

                def rstd_pow(i, v_bn):
                    p.add("gpsimd", lambda e, i=i: e.tensor_scalar(
                        gvE[:, i:i + 1], mvE[:, 2 * i + 1:2 * i + 2],
                        ceps[:, 0:1], None, op0=ALU.add), waits=[("V", v_bn)])
                    p.add("gpsimd", lambda e, i=i: e.tensor_tensor(
                        rstdE[:, 2 * i + 1:2 * i + 2], gvE[:, i:i + 1],
                        cmh[:, 0:1], op=ALU.pow), inc="G")
                    return p.val("G")

                E_x1 = {}
                E_tev = {}
                E_th2 = {}

                def frontA(lt):
                    qoff = lt * 128
                    gt = T0 + lt
                    par = lt % 2
                    p.add("sync", lambda e, gt=gt, par=par: e.dma_start(
                        x0t[:, par, :], XQ[gt * 128:(gt + 1) * 128, :]),
                        waits=[("V", E_x1.get(lt - 2, EPV.get("x1p", 0))),
                               ("fF", f_all)],
                        inc="qS", amt=16)
                    EPV[("qx0", lt)] = p.val("qS")
                    ibc = invl_bc[:, qoff:qoff + 128].unsqueeze(1).broadcast_to(
                        [128, DC, 128])
                    p.add("vector", lambda e, qoff=qoff, ibc=ibc: e.tensor_tensor(
                        Obf[:, :, :], Oacc[:, :, qoff:qoff + 128], ibc,
                        op=ALU.mult),
                        waits=[("T", EPV.get("outproj", 0)), ("V", v_invl),
                               ("fF", f_all)],
                        inc="V")
                    v_obf = p.val("V")
                    ows = [("V", v_obf), ("V", EPV.get("attnsb", 0))]
                    for d in range(DC):
                        for hh in range(2):
                            p.add("tensor", lambda e, d=d, hh=hh: e.matmul(
                                peA[:, hh, :], Obf[:, d, :],
                                woutT[:, d, hh * 512:(hh + 1) * 512],
                                start=(d == 0), stop=(d == DC - 1)),
                                waits=(ows if (d == 0 and hh == 0) else []),
                                inc=("T" if (d == DC - 1 and hh == 1) else None))
                    EPV["outproj"] = p.val("T")

                def frontB(lt):
                    gt = T0 + lt
                    par = lt % 2
                    t_op = EPV["outproj"]
                    for hh in range(2):
                        p.add("vector", lambda e, hh=hh: e.tensor_tensor(
                            attn_sb[:, hh * 512:(hh + 1) * 512], peA[:, hh, :],
                            biasb[:, hh * 512:(hh + 1) * 512], op=ALU.add),
                            waits=([("T", t_op)] if hh == 0 else []),
                            inc=("V" if hh == 1 else None))
                    EPV["attnsb"] = p.val("V")
                    if dbg:
                        p.add("gpsimd", lambda e, gt=gt: e.dma_start(
                            DBGA[gt * 128:(gt + 1) * 128, :], attn_sb[:, :]),
                            waits=[("V", EPV["attnsb"])], inc="qOut", amt=16)
                        out_n[0] += 1
                    p.add("vector", lambda e: e.bn_stats(bnstE[:, 0:6], attn_sb[:, 0:512]))
                    p.add("vector", lambda e: e.bn_stats(bnstE[:, 6:12], attn_sb[:, 512:1024]), inc="V")
                    _bv = p.val("V")
                    p.add("vector", lambda e: e.bn_aggr(mvE[:, 0:2], bnstE[:, :]),
                          waits=[("V", _bv)], inc="V")
                    v_bn1 = p.val("V")
                    g_r1 = rstd_pow(0, v_bn1)
                    p.add("vector", lambda e: e.tensor_scalar(
                        ffn_sb[:, :], attn_sb[:, :], mvE[:, 0:1], rstdE[:, 1:2],
                        op0=ALU.subtract, op1=ALU.mult), waits=[("G", g_r1)])
                    p.add("vector", lambda e, par=par: e.tensor_tensor(
                        x1t[:, par, :], ffn_sb[:, :], x0t[:, par, :], op=ALU.add),
                        waits=[("qS", EPV[("qx0", lt)])], inc="V")
                    E_x1[lt] = p.val("V")
                    EPV["x1p"] = E_x1[lt]
                    if dbg:
                        p.add("gpsimd", lambda e, gt=gt, par=par: e.dma_start(
                            DBGX1[gt * 128:(gt + 1) * 128, :], x1t[:, par, :]),
                            waits=[("V", E_x1[lt])], inc="qOut", amt=16)
                        out_n[0] += 1
                    p.add("vector", lambda e, par=par: e.bn_stats(bnstE[:, 0:6], x1t[:, par, 0:512]))
                    p.add("vector", lambda e, par=par: e.bn_stats(bnstE[:, 6:12], x1t[:, par, 512:1024]), inc="V")
                    _bv = p.val("V")
                    p.add("vector", lambda e: e.bn_aggr(mvE[:, 2:4], bnstE[:, :]),
                          waits=[("V", _bv)], inc="V")
                    v_bn2 = p.val("V")
                    g_r2 = rstd_pow(1, v_bn2)
                    p.add("vector", lambda e, par=par: e.tensor_scalar(
                        h2t[:, :], x1t[:, par, :], mvE[:, 2:3], rstdE[:, 3:4],
                        op0=ALU.subtract, op1=ALU.mult),
                        waits=[("G", g_r2), ("T", E_th2.get(lt - 1, 0))], inc="V")
                    EPV["vh2"] = p.val("V")

                def frontC_T(lt):
                    tws = [("V", EPV["vh2"]), ("V", E_tev.get(lt - 2, 0))]
                    for c in range(DC):
                        p.add("tensor", lambda e, c=c, lt=lt: _tp(e,
                            peT[:, lt % 2, c, :], h2t[:, c * 128:(c + 1) * 128],
                            eyeb[:, :]), waits=(tws if c == 0 else []),
                            inc=("T" if c == DC - 1 else None))
                    E_th2[lt] = p.val("T")

                def frontC_V(lt):
                    p.add("vector", lambda e, lt=lt: e.tensor_copy(
                        h2f[:, :, 1:129], peT[:, lt % 2, :, :]),
                        waits=[("T", E_th2[lt])], inc="V")
                    E_tev[lt] = p.val("V")
                    h2tv = h2t.ap().rearrange("p (c k) -> p c k", k=128)
                    p.add("vector", lambda e, h2tv=h2tv: e.tensor_tensor(
                        h2tv[:, :, :], h2f[:, :, 1:129], h2f[:, :, 0:128],
                        op=ALU.subtract))
                    p.add("vector", lambda e, h2tv=h2tv: e.tensor_tensor(
                        xr[:, :, :], h2tv[:, :, :], mure[:, :, :], op=ALU.mult))
                    p.add("vector", lambda e: e.tensor_tensor(
                        xr[:, :, :], xr[:, :, :], h2f[:, :, 0:128], op=ALU.add))
                    p.add("vector", lambda e, h2tv=h2tv: e.tensor_tensor(
                        xk[:, :, :], h2tv[:, :, :], muke[:, :, :], op=ALU.mult))
                    p.add("vector", lambda e: e.tensor_tensor(
                        xk[:, :, :], xk[:, :, :], h2f[:, :, 0:128], op=ALU.add))
                    p.add("vector", lambda e: e.tensor_copy(
                        h2f[:, :, 0:1], h2f[:, :, 128:129]), inc="V")
                    EPV["xrk"] = p.val("V")

                def backT1(lt):
                    kws = [("V", EPV["xrk"]), ("S", EPV.get("sigmoid", 0))]
                    t_kf = []
                    for b in range(2):
                        for j in range(4):
                            dout = 4 * b + j
                            for c in range(DC):
                                p.add("tensor",
                                      lambda e, b=b, j=j, dout=dout, c=c: e.matmul(
                                          peK[:, b, j * 128:(j + 1) * 128],
                                          wkT[:, c, dout * 128:(dout + 1) * 128],
                                          xk[:, c, :],
                                          start=(j == 0 and c == 0), stop=(c == DC - 1),
                                          skip_group_check=True),
                                      waits=(kws if (b == 0 and j == 0 and c == 0)
                                             else []),
                                      inc=("T" if (j == 3 and c == DC - 1) else None))
                        t_kf.append(p.val("T"))
                    EPV["tkf"] = t_kf

                def backSV1(lt):
                    t_kf = EPV["tkf"]
                    for b in range(2):
                        p.add("scalar", lambda e, b=b: e.activation(
                            kr_sb[:, b, :], peK[:, b, :], AF.Relu),
                            waits=[("T", t_kf[b])], inc=("S" if b == 1 else None))
                    EPV["relu"] = p.val("S")
                    for b in range(2):
                        p.add("scalar", lambda e, b=b: e.activation(
                            k_act[:, b * 4:(b + 1) * 4, :], kr_sb[:, b, :],
                            AF.Square), inc=("S" if b == 1 else None))
                    EPV["ka"] = ("S", p.val("S"))

                def backT2(lt):
                    rws = [("V", EPV["xrk"]), ("S", EPV["relu"]), EPV["ka"]]
                    for c in range(DC):
                        for hh in range(2):
                            p.add("tensor", lambda e, c=c, hh=hh: e.matmul(
                                peK[:, hh, :], xr[:, c, :],
                                wrT[:, c, hh * 512:(hh + 1) * 512],
                                start=(c == 0), stop=(c == DC - 1)),
                                waits=(rws if (c == 0 and hh == 0) else []),
                                inc=("T" if (c == DC - 1 and hh == 1) else None))
                    t_r = p.val("T")
                    for hh in range(2):
                        p.add("scalar", lambda e, hh=hh: e.activation(
                            r_sb[:, hh * 512:(hh + 1) * 512], peK[:, hh, :],
                            AF.Sigmoid),
                            waits=([("T", t_r)] if hh == 0 else []),
                            inc=("S" if hh == 1 else None))
                    EPV["sigmoid"] = p.val("S")

                def backT3(lt):
                    vws = [EPV["ka"], ("V", EPV.get("rv", 0))]
                    for c in range(DC):
                        for hh in range(2):
                            p.add("tensor", lambda e, c=c, hh=hh: e.matmul(
                                peV[:, hh, :], k_act[:, c, :],
                                wvT[:, c, hh * 512:(hh + 1) * 512],
                                start=(c == 0), stop=(c == DC - 1)),
                                waits=(vws if (c == 0 and hh == 0) else []),
                                inc=("T" if (c == DC - 1 and hh == 1) else None))
                    EPV["tv"] = p.val("T")

                def backV(lt):
                    gt = T0 + lt
                    par = lt % 2
                    t_v = EPV["tv"]
                    s_sig = EPV["sigmoid"]
                    for hh in range(2):
                        p.add("vector", lambda e, hh=hh: e.tensor_tensor(
                            ffn_sb[:, hh * 512:(hh + 1) * 512],
                            r_sb[:, hh * 512:(hh + 1) * 512], peV[:, hh, :],
                            op=ALU.mult),
                            waits=([("T", t_v), ("S", s_sig)] if hh == 0 else []),
                            inc=("V" if hh == 1 else None))
                    EPV["rv"] = p.val("V")
                    p.add("vector", lambda e: e.bn_stats(bnstE[:, 0:6], ffn_sb[:, 0:512]))
                    p.add("vector", lambda e: e.bn_stats(bnstE[:, 6:12], ffn_sb[:, 512:1024]), inc="V")
                    _bv = p.val("V")
                    p.add("vector", lambda e: e.bn_aggr(mvE[:, 4:6], bnstE[:, :]),
                          waits=[("V", _bv)], inc="V")
                    v_bn3 = p.val("V")
                    g_r3 = rstd_pow(2, v_bn3)
                    p.add("vector", lambda e: e.tensor_scalar(
                        attn_sb[:, :], ffn_sb[:, :], mvE[:, 4:5], rstdE[:, 5:6],
                        op0=ALU.subtract, op1=ALU.mult), waits=[("G", g_r3)])
                    p.add("vector", lambda e, par=par: e.tensor_tensor(
                        x2sb[:, :], attn_sb[:, :], x1t[:, par, :], op=ALU.add),
                        waits=[("qOut", 16 * out_n[0])], inc="V")
                    v_x2 = p.val("V")
                    p.add("gpsimd", lambda e, gt=gt: e.dma_start(
                        Y[gt * 128:(gt + 1) * 128, :], x2sb[:, :]),
                        waits=[("V", v_x2)], inc="qOut", amt=16)
                    out_n[0] += 1

                if NTE > 0:
                    frontA(0)
                    frontB(0)
                    frontC_T(0)
                    frontC_V(0)
                for lt in range(NTE):
                    backT1(lt)
                    if lt + 1 < NTE:
                        frontA(lt + 1)
                    backSV1(lt)
                    if lt + 1 < NTE:
                        frontB(lt + 1)
                    backT2(lt)
                    backT3(lt)
                    if lt + 1 < NTE:
                        frontC_T(lt + 1)
                    backV(lt)
                    if lt + 1 < NTE:
                        frontC_V(lt + 1)

                p.emit_into(blk)
            blkg.__exit__(None, None, None)

        p.add("sync", lambda e: e.nop(), waits=[("qOut", 16 * out_n[0])])
        p.block()

    return nc


_NC = None


def kernel(L0_attn_x, L0_attn_weight_qkv, L0_attn_out_w, L0_attn_out_b,
           L0_ffn_W_r, L0_ffn_W_k, L0_ffn_W_v, L0_ffn_mu_r, L0_ffn_mu_k):
    global _NC
    x = np.asarray(L0_attn_x, dtype=np.float32)
    wqkv = np.ascontiguousarray(np.asarray(L0_attn_weight_qkv, dtype=np.float32))
    wout = np.ascontiguousarray(np.asarray(L0_attn_out_w, dtype=np.float32))
    outb = np.ascontiguousarray(np.asarray(L0_attn_out_b, dtype=np.float32))
    wr = np.ascontiguousarray(np.asarray(L0_ffn_W_r, dtype=np.float32))
    wk = np.ascontiguousarray(np.asarray(L0_ffn_W_k, dtype=np.float32))
    wv = np.ascontiguousarray(np.asarray(L0_ffn_W_v, dtype=np.float32))
    mur = np.ascontiguousarray(np.asarray(L0_ffn_mu_r, dtype=np.float32))
    muk = np.ascontiguousarray(np.asarray(L0_ffn_mu_k, dtype=np.float32))
    eye = np.eye(128, dtype=np.float32)

    if _NC is None:
        _NC = build()

    in_maps = []
    for c in range(8):
        b, h = c // 2, c % 2
        r0 = 0 if h == 0 else NSEQ - NQ
        in_maps.append({
            "XQ": np.ascontiguousarray(x[b, r0:r0 + NQ]),
            "XF": np.ascontiguousarray(x[b]),
            "WQKV": wqkv, "WOUT": wout, "OUTB": outb,
            "WR": wr, "WK": wk, "WV": wv, "MUR": mur, "MUK": muk, "EYE": eye,
        })
    res = run_bass_kernel_spmd(_NC, in_maps, core_ids=list(range(8)))
    out = np.empty((4, NSEQ, D), dtype=np.float32)
    for c in range(8):
        b, h = c // 2, c % 2
        y = np.asarray(res.results[c]["Y"])
        if h == 0:
            out[b, 0:NQ] = y
        else:
            out[b, NQ:NSEQ] = y[NQ - (NSEQ - NQ):NQ]
    return out


# version 16
# speedup vs baseline: 1.0474x; 1.0474x over previous
import contextlib

import numpy as np

import concourse.bass as bass
import concourse.mybir as mybir
from concourse.bass_utils import run_bass_kernel_spmd

F32 = mybir.dt.float32
BF16 = mybir.dt.bfloat16
AF = mybir.ActivationFunctionType
ALU = mybir.AluOpType

D = 1024
DC = 8
NSEQ = 8192
NQ = 4224
NKT = 64
SUPERS = [12, 12, 12, 12, 12, 4]
CHUNK_T = [6, 6, 6, 6, 6, 3]
RING = 15
QG = 256
EPS = 1e-5
EXPB = -10.0


class Plan:

    def __init__(self, nc, stack, sem_names):
        self.nc = nc
        self.ops = []
        self.cnt = {}
        self.sem = {}
        for n in sem_names:
            self.sem[n] = stack.enter_context(nc.semaphore(n))
            self.cnt[n] = 0

    def add(self, eng, emit, waits=(), inc=None, amt=1):
        ws = []
        for sname, v in waits:
            assert v <= self.cnt[sname], (sname, v, self.cnt[sname])
            if v > 0:
                ws.append((sname, v))
        if inc is not None:
            self.cnt[inc] += amt
        self.ops.append((eng, emit, tuple(ws), inc, amt))
        return self.cnt.get(inc)

    def val(self, name):
        return self.cnt[name]

    def emit_into(self, blk):
        by_eng = {}
        for op in self.ops:
            by_eng.setdefault(op[0], []).append(op)
        self.ops = []
        for engname in ("sync", "tensor", "scalar", "vector", "gpsimd"):
            ops = by_eng.get(engname)
            if not ops:
                continue

            def body(eng, _ops=ops, _p=self):
                for _, emit, ws, inc, amt in _ops:
                    for sname, v in ws:
                        eng.wait_ge(_p.sem[sname], v)
                    last = emit(eng)
                    if inc is not None:
                        last.then_inc(_p.sem[inc], amt)

            getattr(blk, engname)(body)

    def block(self):
        with self.nc.Block() as blk:
            self.emit_into(blk)



def _tp(e, out, in_, ident):
    e.nop(cycle_cnt=32)
    r = e.transpose(out, in_, ident)
    e.nop(cycle_cnt=32)
    return r

def build(dbg=False, phases="AFE", ngf=16, ngq=9, nwt=4, skip_bias=False, skip_wqkv=False, epi_tiles=99, epi_skip=(), la=True):
    nc = bass.Bass("TRN2", target_bir_lowering=False)
    SCR = "ExternalOutput" if dbg else "Internal"

    XQ = nc.dram_tensor("XQ", [NQ, D], F32, kind="ExternalInput")
    XF = nc.dram_tensor("XF", [NSEQ, D], F32, kind="ExternalInput")
    WQKV = nc.dram_tensor("WQKV", [D, 3 * D], F32, kind="ExternalInput")
    WOUT = nc.dram_tensor("WOUT", [D, D], F32, kind="ExternalInput")
    OUTB = nc.dram_tensor("OUTB", [D], F32, kind="ExternalInput")
    WR = nc.dram_tensor("WR", [D, D], F32, kind="ExternalInput")
    WK = nc.dram_tensor("WK", [D, D], F32, kind="ExternalInput")
    WV = nc.dram_tensor("WV", [D, D], F32, kind="ExternalInput")
    MUR = nc.dram_tensor("MUR", [D], F32, kind="ExternalInput")
    MUK = nc.dram_tensor("MUK", [D], F32, kind="ExternalInput")
    EYE = nc.dram_tensor("EYE", [128, 128], F32, kind="ExternalInput")
    Y = nc.dram_tensor("Y", [NQ, D], F32, kind="ExternalOutput")
    if dbg:
        DBGW = nc.dram_tensor("DBGW", [4, 128, DC, D], BF16, kind="ExternalOutput")
        DBGC = nc.dram_tensor("DBGC", [6, 128, 8], F32, kind="ExternalOutput")
        DBGN = nc.dram_tensor("DBGN", [NSEQ, D], BF16, kind="ExternalOutput")
        DBGO = nc.dram_tensor("DBGO", [6, 128, DC, 768], F32, kind="ExternalOutput")
        DBGP = nc.dram_tensor("DBGP", [6, 128, 768], F32, kind="ExternalOutput")
        DBGL = nc.dram_tensor("DBGL", [6, 768], F32, kind="ExternalOutput")
        DBGA = nc.dram_tensor("DBGA", [NQ, D], F32, kind="ExternalOutput")
        DBGX1 = nc.dram_tensor("DBGX1", [NQ, D], F32, kind="ExternalOutput")

    KS = nc.dram_tensor("KS", [NKT, 128, DC, 128], BF16, kind=SCR)
    VS = nc.dram_tensor("VS", [NKT, 128, D], BF16, kind=SCR)
    QS = nc.dram_tensor("QS", [DC, 128, NQ], BF16, kind=SCR)
    LS = nc.dram_tensor("LS", [768], F32, kind="Internal")

    sems = (["T", "V", "S", "qS", "qQ", "qWk", "qWv", "qWq", "qWw", "qOut",
             "fS", "fE", "fP", "fO", "fF", "qL", "qWt", "qQ0", "qQ1", "qQ2", "G"] + [f"sl{i}" for i in range(RING)] + [f"slK{i}" for i in range(RING)])

    stack = contextlib.ExitStack()
    with stack:
        p = Plan(nc, stack, sems)

        def sb(name, shape, dt):
            return stack.enter_context(nc.sbuf_tensor(name, shape, dt))

        woutT = sb("woutT", [128, DC, D], BF16)
        wrT = sb("wrT", [128, DC, D], BF16)
        wkT = sb("wkT", [128, DC, D], BF16)
        wvT = sb("wvT", [128, DC, D], BF16)
        biasb = sb("biasb", [128, D], BF16)
        murf = sb("murf", [128, DC], F32)
        mukf = sb("mukf", [128, DC], F32)
        mure = sb("mure", [128, DC, 128], BF16)
        muke = sb("muke", [128, DC, 128], BF16)
        eyef = sb("eyef", [128, 128], F32)
        eyeb = sb("eyeb", [128, 128], BF16)
        ones128 = sb("ones128", [128, 128], BF16)
        cexp = sb("cexp", [128, 1], F32)
        cmh = sb("cmh", [128, 1], F32)
        ceps = sb("ceps", [128, 1], F32)

        with contextlib.ExitStack() as ascope:
            def asb(name, shape, dt):
                return ascope.enter_context(nc.sbuf_tensor(name, shape, dt))

            wqkv_b = asb("wqkv_b", [128, DC, 3 * D], BF16)
            onesrow = asb("onesrow", [128, 128], BF16)
            obmat = asb("obmat", [128, D], BF16)
            wtmp = asb("wtmp", [128, 2, D], BF16)
            xst = asb("xst", [128, 4, D], F32)
            lnb = asb("lnb", [128, 4, D], BF16)
            xT = asb("xT", [128, 2, DC, 512], BF16)
            kst = asb("kst", [128, 2, 512], BF16)
            vst = asb("vst", [128, 2, D], BF16)
            qst = asb("qst", [128, 2, 512], BF16)
            bnst = asb("bnst", [128, 12], F32)
            mvA = asb("mvA", [128, 4, 2], F32)
            rstdA = asb("rstdA", [128, 4, 2], F32)
            pTx = ascope.enter_context(nc.psum_tensor("pTx", [128, 2, DC, 128], BF16))
            pK = ascope.enter_context(nc.psum_tensor("pK", [128, 2, 512], F32))
            pV = ascope.enter_context(nc.psum_tensor("pV", [128, 2, D], F32))

            for wu in range(256):
                p.add("tensor", lambda e: e.matmul(
                    pV[:, 0, 0:512], wqkv_b[:, 0, 0:128], wqkv_b[:, 0, 0:512],
                    start=True, stop=True))
            for wu in range(16):
                p.add("tensor", lambda e: _tp(e, pTx[:, 0, wu % 8, :],
                                              wqkv_b[:, 0, 0:128], wqkv_b[:, 0, 128:256]))
            p.add("tensor", lambda e: e.nop(), inc="T")

            p.add("vector", lambda e: e.memset(cmh[:, :], -0.5))
            p.add("vector", lambda e: e.memset(cexp[:, :], EXPB))
            p.add("vector", lambda e: e.memset(ceps[:, :], EPS))
            p.add("vector", lambda e: e.memset(onesrow[:, :], 0.0))
            p.add("vector", lambda e: e.memset(onesrow[0:1, :], 1.0))
            p.add("vector", lambda e: e.memset(ones128[:, :], 1.0))
            p.add("vector", lambda e: e.memset(obmat[:, :], 0.0), inc="V")
            v_const = p.val("V")

            p.add("sync", lambda e: e.dma_start(eyef[:, :], EYE[:, :]), inc="qS", amt=16)
            p.add("sync", lambda e: e.dma_start(
                murf[:, :], MUR.ap().rearrange("(c p) -> p c", p=128),
                allow_slow_non_contiguous=True), inc="qS", amt=16)
            p.add("sync", lambda e: e.dma_start(
                mukf[:, :], MUK.ap().rearrange("(c p) -> p c", p=128),
                allow_slow_non_contiguous=True), inc="qS", amt=16)
            q_small = p.val("qS")
            p.add("gpsimd", lambda e: e.dma_start(
                obmat[0:1, :], OUTB.ap().rearrange("(a d) -> a d", a=1)),
                waits=[("V", v_const)], inc="qWw", amt=16)

            for d in range(DC if not skip_wqkv else 0):
                p.add("gpsimd", lambda e, d=d: e.dma_start(
                    wqkv_b[:, d, :], WQKV[d * 128:(d + 1) * 128, :]),
                    inc="qWw", amt=16)
            q_wqkv = p.val("qWw")

            p.add("vector", lambda e: e.tensor_copy(eyeb[:, :], eyef[:, :]),
                  waits=[("qS", q_small)], inc="V")
            v_eyeb = p.val("V")
            if not skip_bias:
                p.add("tensor", lambda e: e.matmul(
                    pV[:, 0, 0:512], onesrow[:, :], obmat[:, 0:512], start=True, stop=True),
                    waits=[("qS", q_small), ("qWw", p.val("qWw")), ("V", v_const)])
                p.add("tensor", lambda e: e.matmul(
                    pK[:, 0, 0:512], onesrow[:, :], obmat[:, 512:1024],
                    start=True, stop=True), inc="T")
                t_bias = p.val("T")
                p.add("vector", lambda e: e.tensor_copy(biasb[:, 0:512], pV[:, 0, 0:512]),
                      waits=[("T", t_bias)])
                p.add("vector", lambda e: e.tensor_copy(
                    biasb[:, 512:1024], pK[:, 0, 0:512]), inc="V")

            for c in range(DC):
                p.add("vector", lambda e, c=c: e.tensor_scalar(
                    mure[:, c, :], ones128[:, :], murf[:, c:c + 1], None,
                    op0=ALU.mult), waits=([("qS", q_small)] if c == 0 else []))
            for c in range(DC):
                p.add("vector", lambda e, c=c: e.tensor_scalar(
                    muke[:, c, :], ones128[:, :], mukf[:, c:c + 1], None,
                    op0=ALU.mult), inc=("V" if c == DC - 1 else None))

            wt_ev = {}
            for wi, (Wd, Wt) in enumerate(
                    [(WOUT, woutT), (WR, wrT), (WK, wkT), (WV, wvT)][:nwt]):
                for ch in range(DC):
                    n = wi * DC + ch
                    par = n % 2
                    lws = [("V", wt_ev[n - 2])] if n - 2 in wt_ev else []
                    p.add("gpsimd", lambda e, Wd=Wd, ch=ch, par=par: e.dma_start(
                        wtmp[:, par, :], Wd[ch * 128:(ch + 1) * 128, :]),
                        waits=lws, inc="qWt", amt=16)
                    qv = p.val("qWt")
                    tws = [("qWt", qv), ("V", v_eyeb)]
                    if n - 2 in wt_ev:
                        tws.append(("V", wt_ev[n - 2]))
                    for bi in range(DC):
                        p.add("tensor", lambda e, par=par, bi=bi: _tp(e, 
                            pTx[:, par, bi, :],
                            wtmp[:, par, bi * 128:(bi + 1) * 128], eyeb[:, :]),
                            waits=(tws if bi == 0 else []),
                            inc=("T" if bi == DC - 1 else None))
                    tv = p.val("T")
                    p.add("vector", lambda e, Wt=Wt, par=par, ch=ch: e.tensor_copy(
                        Wt[:, :, ch * 128:(ch + 1) * 128], pTx[:, par, :, :]),
                        waits=[("T", tv)], inc="V")
                    wt_ev[n] = p.val("V")

            last_ln = {}
            last_tp = {}
            last_ev = {}
            last_mm_grp = {}
            kev = {}
            vev = {}
            qev = {}
            nwk = [0]
            nwv = [0]
            nwq = [0]
            tile_seq = [0]
            grp_seq = [0]
            KS_p = KS.ap().rearrange("t p c k -> p t c k")

            pend_mm = []

            def emit_group(SRC, t0, ntl, kinds):
                g = grp_seq[0]
                grp_seq[0] += 1
                gpar = g % 2
                tw = ntl * 128
                svs = {}

                def stage_a(j, ts, par, tt):
                    lws = [("V", last_ln[ts - 4])] if ts - 4 in last_ln else []
                    p.add("sync", lambda e, SRC=SRC, tt=tt, par=par: e.dma_start(
                        xst[:, par, :], SRC[tt * 128:(tt + 1) * 128, :]),
                        waits=lws, inc="qS", amt=16)
                    qv = p.val("qS")
                    p.add("vector", lambda e, par=par: e.bn_stats(
                        bnst[:, 0:6], xst[:, par, 0:512]), waits=[("qS", qv)])
                    p.add("vector", lambda e, par=par: e.bn_stats(
                        bnst[:, 6:12], xst[:, par, 512:1024]), inc="V")
                    _bv = p.val("V")
                    p.add("vector", lambda e, par=par: e.bn_aggr(
                        mvA[:, par, :], bnst[:, :]), waits=[("V", _bv)], inc="V")
                    vv = p.val("V")
                    p.add("scalar", lambda e, par=par: e.activation(
                        rstdA[:, par, 0:1], mvA[:, par, 1:2], AF.Ln,
                        bias=ceps[:, 0:1]), waits=[("V", vv)], inc="S")
                    _sv = p.val("S")
                    p.add("scalar", lambda e, par=par: e.activation(
                        rstdA[:, par, 1:2], rstdA[:, par, 0:1], AF.Exp,
                        scale=-0.5), waits=[("S", _sv)], inc="S")
                    svs[ts] = p.val("S")

                def norm(j, ts, par):
                    nws = [("S", svs[ts])]
                    if ts - 4 in last_tp:
                        nws.append(("T", last_tp[ts - 4]))
                    p.add("vector", lambda e, par=par: e.tensor_scalar(
                        lnb[:, par, :], xst[:, par, :], mvA[:, par, 0:1],
                        rstdA[:, par, 1:2],
                        op0=ALU.subtract, op1=ALU.mult), waits=nws, inc="V")
                    last_ln[ts] = p.val("V")
                    if dbg and SRC is XF:
                        p.add("gpsimd", lambda e, par=par, ts=ts: e.dma_start(
                            DBGN[ts * 128:(ts + 1) * 128, :], lnb[:, par, :]),
                            waits=[("V", last_ln[ts])], inc="qWw", amt=16)

                def tp_unit(j, ts, par):
                    pp = ts % 2
                    tws = [("V", last_ln[ts])]
                    if ts - 2 in last_ev:
                        tws.append(("V", last_ev[ts - 2]))
                    for c in range(DC):
                        p.add("tensor", lambda e, par=par, pp=pp, c=c: _tp(e,
                            pTx[:, pp, c, :], lnb[:, par, c * 128:(c + 1) * 128],
                            eyeb[:, :]), waits=(tws if c == 0 else []),
                            inc=("T" if c == DC - 1 else None))
                    last_tp[ts] = p.val("T")
                    ews = [("T", last_tp[ts])]
                    if g - 2 in last_mm_grp:
                        ews.append(("T", last_mm_grp[g - 2]))
                    p.add("vector", lambda e, pp=pp, gpar=gpar, j=j: e.tensor_copy(
                        xT[:, gpar, :, j * 128:(j + 1) * 128], pTx[:, pp, :, :]),
                        waits=ews, inc="V")
                    last_ev[ts] = p.val("V")

                tss = []
                for j in range(ntl):
                    ts = tile_seq[0]
                    tile_seq[0] += 1
                    tss.append((j, ts, ts % 4))
                    stage_a(j, ts, ts % 4, t0 + j)
                for t in tss:
                    norm(*t)
                tp_units = [(lambda t=t: tp_unit(*t)) for t in tss]

                if pend_mm:
                    pgid, punits = pend_mm.pop()
                    mi = ti = 0
                    while mi < len(punits) or ti < len(tp_units):
                        for _ in range(2):
                            if mi < len(punits):
                                punits[mi]()
                                mi += 1
                        if ti < len(tp_units):
                            tp_units[ti]()
                            ti += 1
                    last_mm_grp[pgid] = p.val("T")
                else:
                    for f in tp_units:
                        f()

                def xtr():
                    return [("V", last_ev[tss[-1][1]]), ("qWw", q_wqkv)]

                mm_units = []
                if "K" in kinds:
                    for c in range(DC):
                        def mk_k(c=c):
                            ws = xtr() if c == 0 else []
                            if nwk[0] - 2 in kev:
                                ws.append(("S", kev[nwk[0] - 2]))
                            for d in range(DC):
                                p.add("tensor",
                                      lambda e, c=c, d=d, gpar=gpar, tw=tw: e.matmul(
                                          pK[:, c % 2, 0:tw],
                                          wqkv_b[:, d, D + c * 128:D + (c + 1) * 128],
                                          xT[:, gpar, d, 0:tw],
                                          start=(d == 0), stop=(d == DC - 1)),
                                      waits=(ws if d == 0 else []),
                                      inc=("T" if d == DC - 1 else None))
                            tv = p.val("T")
                            p.add("scalar", lambda e, c=c, tw=tw: e.activation(
                                kst[:, c % 2, 0:tw], pK[:, c % 2, 0:tw], AF.Copy),
                                waits=[("T", tv), ("qWk", 16 * max(0, nwk[0] - 1))],
                                inc="S")
                            kev[nwk[0]] = p.val("S")
                            sv = p.val("S")
                            p.add("gpsimd",
                                  lambda e, c=c, t0=t0, ntl=ntl: e.dma_start(
                                      KS_p[:, t0:t0 + ntl, c, :],
                                      kst[:, c % 2, 0:ntl * 128].rearrange(
                                          "p (t k) -> p t k", k=128)),
                                  waits=[("S", sv)], inc="qWk", amt=16)
                            nwk[0] += 1
                        mm_units.append(mk_k)
                if "V" in kinds:
                    for j in range(ntl):
                        def mk_v(j=j):
                            ws = xtr() if j == 0 else []
                            if nwv[0] - 2 in vev:
                                ws.append(("S", vev[nwv[0] - 2]))
                            for d in range(DC):
                                for hh in range(2):
                                    p.add("tensor",
                                          lambda e, j=j, d=d, hh=hh, gpar=gpar: e.matmul(
                                              pV[:, j % 2, hh * 512:(hh + 1) * 512],
                                              xT[:, gpar, d, j * 128:(j + 1) * 128],
                                              wqkv_b[:, d,
                                                     2 * D + hh * 512:2 * D + (hh + 1) * 512],
                                              start=(d == 0), stop=(d == DC - 1)),
                                          waits=(ws if (d == 0 and hh == 0) else []),
                                          inc=("T" if (d == DC - 1 and hh == 1) else None))
                            tv = p.val("T")
                            p.add("scalar", lambda e, j=j: e.activation(
                                vst[:, j % 2, :], pV[:, j % 2, :], AF.Copy),
                                waits=[("T", tv), ("qWv", 16 * max(0, nwv[0] - 1))],
                                inc="S")
                            vev[nwv[0]] = p.val("S")
                            sv = p.val("S")
                            p.add("gpsimd", lambda e, j=j, t0=t0: e.dma_start(
                                VS[t0 + j, :, :], vst[:, j % 2, :]),
                                waits=[("S", sv)], inc="qWv", amt=16)
                            nwv[0] += 1
                        mm_units.append(mk_v)
                if "Q" in kinds:
                    for c in range(DC):
                        def mk_q(c=c):
                            ws = xtr() if c == 0 else []
                            if nwq[0] - 2 in qev:
                                ws.append(("S", qev[nwq[0] - 2]))
                            for d in range(DC):
                                p.add("tensor",
                                      lambda e, c=c, d=d, gpar=gpar, tw=tw: e.matmul(
                                          pK[:, c % 2, 0:tw],
                                          wqkv_b[:, d, c * 128:(c + 1) * 128],
                                          xT[:, gpar, d, 0:tw],
                                          start=(d == 0), stop=(d == DC - 1)),
                                      waits=(ws if d == 0 else []),
                                      inc=("T" if d == DC - 1 else None))
                            tv = p.val("T")
                            p.add("scalar", lambda e, c=c, tw=tw: e.activation(
                                qst[:, c % 2, 0:tw], pK[:, c % 2, 0:tw], AF.Copy,
                                scale=1.0 / 32.0),
                                waits=[("T", tv), ("qWq", 16 * max(0, nwq[0] - 1))],
                                inc="S")
                            qev[nwq[0]] = p.val("S")
                            sv = p.val("S")
                            p.add("gpsimd", lambda e, c=c, t0=t0, tw=tw: e.dma_start(
                                QS[c, :, t0 * 128:t0 * 128 + tw], qst[:, c % 2, 0:tw]),
                                waits=[("S", sv)], inc="qWq", amt=16)
                            nwq[0] += 1
                        mm_units.append(mk_q)
                pend_mm.append((g, mm_units))

            for g in range(min(16, ngf)):
                emit_group(XF, 4 * g, 4, ("K", "V"))
            for g in range(min(8, max(0, ngq - 1))):
                emit_group(XQ, 4 * g, 4, ("Q",))
            if ngq >= 9:
                emit_group(XQ, 32, 1, ("Q",))
            while pend_mm:
                pgid, punits = pend_mm.pop()
                for f in punits:
                    f()
                last_mm_grp[pgid] = p.val("T")


            p.block()

        scr_k = 16 * nwk[0]
        scr_v = 16 * nwv[0]
        scr_q = 16 * nwq[0]

        if phases == "A":
            p.add("sync", lambda e: e.nop(), waits=[("qWk", scr_k)])
            p.add("scalar", lambda e: e.nop(), waits=[("qWv", scr_v), ("qWq", scr_q)])
            p.add("gpsimd", lambda e: e.dma_start(Y[0:128, :], biasb[:, :]),
                  inc="qOut", amt=16)
            p.add("sync", lambda e: e.nop(), waits=[("qOut", 16)])
            p.block()
            return nc

        Oacc = sb("Oacc", [128, DC, 768], F32)
        Pacc = sb("Pacc", [128, 768], F32)
        PaccB = sb("PaccB", [128, 768], BF16)
        h2f = sb("h2f", [128, DC, 129], BF16)
        invl_bc = sb("invl_bc", [128, 768], F32)
        invl_cols = sb("invl_cols", [128, 8], F32)
        Kring = sb("Kring", [128, RING, DC, 128], BF16)
        Vring = sb("Vring", [128, RING, D], BF16)
        QS_p = QS.ap().rearrange("c p t -> p c t")

        kload_n = [0]
        k_rel = {}
        slot_round = [0] * RING
        slot_wait = {}
        super_fO = {}
        q_wait = {}
        n_exp = [0]
        out_n = [0]
        EPV = {}

        def super_of(kt):
            acc = 0
            for si, sz in enumerate(SUPERS):
                acc += sz
                if kt < acc:
                    return si
            raise AssertionError(kt)

        def kv_guard(gkt):
            prev = gkt - RING
            if prev < 0:
                return 0
            return super_fO[(prev // NKT, super_of(prev % NKT))]

        def pump_loads(ci, upto):
            while kload_n[0] < ci * NKT + min(upto, NKT):
                gkt = kload_n[0]
                kt = gkt % NKT
                slot = gkt % RING
                g = kv_guard(gkt)
                gk = k_rel.get(gkt - RING, 0)
                p.add("sync", lambda e, kt=kt, slot=slot: e.dma_start(
                    Kring[:, slot, :, :], KS[kt, :, :, :]),
                    waits=[("fE", gk)], inc=f"slK{slot}", amt=16)
                vq = "scalar" if gkt % 2 == 0 else "gpsimd"
                p.add(vq, lambda e, kt=kt, slot=slot: e.dma_start(
                    Vring[:, slot, :], VS[kt, :, :]),
                    waits=[("fO", g)], inc=f"sl{slot}", amt=16)
                slot_round[slot] += 1
                slot_wait[(ci, kt)] = 16 * slot_round[slot]
                kload_n[0] += 1

        for ci in range(6):
            T0 = sum(CHUNK_T[:ci])
            nt = CHUNK_T[ci]
            qws = []
            rem = nt * 128
            while rem > 0:
                qws.append(min(QG, rem))
                rem -= qws[-1]

            blkg = nc.Block()
            blk = blkg.__enter__()
            with contextlib.ExitStack() as fscope:
                Qg = fscope.enter_context(
                    nc.sbuf_tensor(f"Qg{ci}", [128, 3, DC, QG], BF16))
                Ptile = fscope.enter_context(
                    nc.sbuf_tensor(f"Ptile{ci}", [128, max(SUPERS), QG], BF16))
                pS = fscope.enter_context(nc.psum_tensor(f"pS{ci}", [128, 2, 512], F32))
                pO = fscope.enter_context(nc.psum_tensor(f"pO{ci}", [128, 4, 512], F32))

                if ci == 0:
                    if dbg:
                        for wi, Wt in enumerate([woutT, wrT, wkT, wvT]):
                            p.add("gpsimd", lambda e, wi=wi, Wt=Wt: e.dma_start(
                                DBGW[wi, :, :, :], Wt[:, :, :]), inc="qL", amt=16)
                    p.add("sync", lambda e: e.nop(), waits=[("qWk", scr_k)])
                    p.add("scalar", lambda e: e.nop(),
                          waits=[("qWv", scr_v), ("qWq", scr_q)])
                    p.add("gpsimd", lambda e: e.nop(), waits=[("qWv", scr_v)])

                qoff = 0
                for gi, qw in enumerate(qws):
                    q0 = T0 * 128 + qoff
                    p.add("scalar", lambda e, gi=gi, q0=q0, qw=qw: e.dma_start(
                        Qg[:, gi, :, 0:qw], QS_p[:, :, q0:q0 + qw]),
                        inc=f"qQ{gi}", amt=16)
                    q_wait[(ci, gi)] = (f"qQ{gi}", p.val(f"qQ{gi}"))
                    qoff += qw

                kt_base = 0
                for si, ssz in enumerate(SUPERS):
                    pump_loads(ci, kt_base + RING)
                    qoff = 0
                    for gi, qw in enumerate(qws):
                        exp_first = True
                        for tl in range(ssz):
                            kt = kt_base + tl
                            ne = n_exp[0]
                            ws = [(f"slK{(ci * NKT + kt) % RING}",
                                   slot_wait[(ci, kt)])]
                            if ne >= 2:
                                ws.append(("fE", ne - 1))
                            if tl == 0:
                                ws.append(q_wait[(ci, gi)])
                            for d in range(DC):
                                p.add("tensor",
                                      lambda e, kt=kt, d=d, gi=gi, qw=qw, ne=ne,
                                      slot=(ci * NKT + kt) % RING: e.matmul(
                                          pS[:, ne % 2, 0:qw],
                                          Kring[:, slot, d, :],
                                          Qg[:, gi, d, 0:qw],
                                          start=(d == 0), stop=(d == DC - 1)),
                                      waits=(ws if d == 0 else []),
                                      inc=("fS" if d == DC - 1 else None))
                            fs = p.val("fS")
                            ews = [("fS", fs)]
                            if exp_first:
                                ews += [("fO", p.val("fO")), ("fP", p.val("fP"))]
                                exp_first = False
                            p.add("scalar", lambda e, tl=tl, qw=qw, ne=ne: e.activation(
                                Ptile[:, tl, 0:qw], pS[:, ne % 2, 0:qw], AF.Exp,
                                bias=cexp[:, 0:1]), waits=ews, inc="fE")
                            fe = p.val("fE")
                            if gi == len(qws) - 1:
                                k_rel[ci * NKT + kt] = fe
                            n_exp[0] += 1
                            first_p = (si == 0 and tl == 0)
                            if first_p:
                                p.add("vector",
                                      lambda e, tl=tl, qw=qw, qoff=qoff: e.tensor_copy(
                                          Pacc[:, qoff:qoff + qw], Ptile[:, tl, 0:qw]),
                                      waits=[("fE", fe)],
                                      inc=("fP" if tl == ssz - 1 else None))
                            else:
                                p.add("vector",
                                      lambda e, tl=tl, qw=qw, qoff=qoff: e.tensor_tensor(
                                          Pacc[:, qoff:qoff + qw],
                                          Pacc[:, qoff:qoff + qw],
                                          Ptile[:, tl, 0:qw], op=ALU.add),
                                      waits=[("fE", fe)],
                                      inc=("fP" if tl == ssz - 1 else None))
                        ows = [("fE", p.val("fE")), ("fF", p.val("fF"))]
                        for tl in range(ssz):
                            kt = kt_base + tl
                            slot = (ci * NKT + kt) % RING
                            for dv in range(DC):
                                st = (tl == 0 and dv % 2 == 0)
                                ow = list(ows) if (tl == 0 and dv == 0) else []
                                if gi == 0 and dv == 0:
                                    ow.append((f"sl{slot}", slot_wait[(ci, kt)]))
                                p.add("tensor",
                                      lambda e, slot=slot, dv=dv, tl=tl, qw=qw,
                                      st=st, lastt=(tl == ssz - 1): e.matmul(
                                          pO[:, dv // 2,
                                             (dv % 2) * 256:(dv % 2) * 256 + qw],
                                          Vring[:, slot, dv * 128:(dv + 1) * 128],
                                          Ptile[:, tl, 0:qw],
                                          start=st, stop=lastt,
                                          skip_group_check=True),
                                      waits=ow,
                                      inc=("fO" if (tl == ssz - 1 and dv == DC - 1)
                                           else None))
                        fo = p.val("fO")
                        fws = [("fO", fo)]
                        for dv in range(DC):
                            if si == 0:
                                p.add("vector",
                                      lambda e, dv=dv, qw=qw, qoff=qoff: e.tensor_copy(
                                          Oacc[:, dv, qoff:qoff + qw],
                                          pO[:, dv // 2,
                                             (dv % 2) * 256:(dv % 2) * 256 + qw]),
                                      waits=(fws if dv == 0 else []),
                                      inc=("fF" if dv == DC - 1 else None))
                            else:
                                p.add("vector",
                                      lambda e, dv=dv, qw=qw, qoff=qoff: e.tensor_tensor(
                                          Oacc[:, dv, qoff:qoff + qw],
                                          Oacc[:, dv, qoff:qoff + qw],
                                          pO[:, dv // 2,
                                             (dv % 2) * 256:(dv % 2) * 256 + qw],
                                          op=ALU.add),
                                      waits=(fws if dv == 0 else []),
                                      inc=("fF" if dv == DC - 1 else None))
                        qoff += qw
                    super_fO[(ci, si)] = p.val("fO")
                    kt_base += ssz

                p.add("vector", lambda e, nt=nt: e.tensor_copy(
                    PaccB[:, 0:nt * 128], Pacc[:, 0:nt * 128]), inc="V")
                v_paccb = p.val("V")
                f_all = p.val("fF")
                p.emit_into(blk)

            if phases == "AF":
                p.add("gpsimd", lambda e, ci=ci: e.dma_start(
                    DBGO[ci, :, :, :], Oacc[:, :, :]), inc="qOut", amt=16)
                p.add("gpsimd", lambda e, ci=ci: e.dma_start(
                    DBGP[ci, :, :], Pacc[:, :]), inc="qOut", amt=16)
                out_n[0] += 2
                p.block()
                continue

            with contextlib.ExitStack() as escope:
                def esb(name, shape, dt):
                    return escope.enter_context(nc.sbuf_tensor(f"{name}{ci}", shape, dt))

                Obf = esb("Obf", [128, DC, 128], BF16)
                attn_sb = esb("attn_sb", [128, D], F32)
                h2t = esb("h2t", [128, D], BF16)
                x0t = esb("x0t", [128, 2, D], F32)
                x1t = esb("x1t", [128, 2, D], F32)
                xr = esb("xr", [128, DC, 128], BF16)
                xk = esb("xk", [128, DC, 128], BF16)
                kr_sb = esb("kr_sb", [128, 2, 512], BF16)
                k_act = esb("k_act", [128, DC, 128], BF16)
                r_sb = esb("r_sb", [128, D], BF16)
                ffn_sb = esb("ffn_sb", [128, D], F32)
                x2sb = esb("x2sb", [128, D], F32)
                bnstE = esb("bnstE", [128, 12], F32)
                gvE = esb("gvE", [128, 4], F32)
                mvE = esb("mvE", [128, 6], F32)
                rstdE = esb("rstdE", [128, 6], F32)
                peA = escope.enter_context(nc.psum_tensor(f"peA{ci}", [128, 2, 512], F32))
                peT = escope.enter_context(
                    nc.psum_tensor(f"peT{ci}", [128, 2, 8, 128], BF16))
                peK = escope.enter_context(nc.psum_tensor(f"peK{ci}", [128, 2, 512], F32))
                peV = escope.enter_context(nc.psum_tensor(f"peV{ci}", [128, 2, 512], F32))

                if ci == 0:
                    p.add("vector", lambda e: e.memset(h2f[:, :, 0:1], 0.0), inc="V")

                ncols = nt * 128
                for j, w in enumerate(range(0, ncols, 512)):
                    lw = min(512, ncols - w)
                    p.add("tensor", lambda e, j=j, w=w, lw=lw: e.matmul(
                        peA[:, j, 0:lw], ones128[:, :], PaccB[:, w:w + lw],
                        start=True, stop=True),
                        waits=([("V", v_paccb), ("fF", f_all)] if j == 0 else []),
                        inc=("T" if w + 512 >= ncols else None))
                t_l = p.val("T")
                for j, w in enumerate(range(0, ncols, 512)):
                    lw = min(512, ncols - w)
                    p.add("vector", lambda e, j=j, w=w, lw=lw: e.reciprocal(
                        invl_bc[:, w:w + lw], peA[:, j, 0:lw]),
                        waits=([("T", t_l)] if j == 0 else []),
                        inc=("V" if w + 512 >= ncols else None))
                v_invl = p.val("V")
                if ci + 1 < 6:
                    pump_loads(ci + 1, RING)
                if dbg:
                    p.add("gpsimd", lambda e, ci=ci: e.dma_start(
                        DBGO[ci, :, :, :], Oacc[:, :, :]), inc="qL", amt=16)
                if dbg:
                    p.add("gpsimd", lambda e, ci=ci, ncols=ncols: e.dma_start(
                        DBGL.ap().rearrange("c t -> c t")[ci:ci + 1, 0:ncols],
                        invl_bc[0:1, 0:ncols]),
                        waits=[("V", v_invl)], inc="qL", amt=16)

                NTE = min(nt, epi_tiles)

                def rstd_pow(i, v_bn):
                    p.add("gpsimd", lambda e, i=i: e.tensor_scalar(
                        gvE[:, i:i + 1], mvE[:, 2 * i + 1:2 * i + 2],
                        ceps[:, 0:1], None, op0=ALU.add), waits=[("V", v_bn)])
                    p.add("gpsimd", lambda e, i=i: e.tensor_tensor(
                        rstdE[:, 2 * i + 1:2 * i + 2], gvE[:, i:i + 1],
                        cmh[:, 0:1], op=ALU.pow), inc="G")
                    return p.val("G")

                E_x1 = {}
                E_tev = {}
                E_th2 = {}

                def frontA(lt):
                    qoff = lt * 128
                    gt = T0 + lt
                    par = lt % 2
                    p.add("sync", lambda e, gt=gt, par=par: e.dma_start(
                        x0t[:, par, :], XQ[gt * 128:(gt + 1) * 128, :]),
                        waits=[("V", E_x1.get(lt - 2, EPV.get("x1p", 0))),
                               ("fF", f_all)],
                        inc="qS", amt=16)
                    EPV[("qx0", lt)] = p.val("qS")
                    ibc = invl_bc[:, qoff:qoff + 128].unsqueeze(1).broadcast_to(
                        [128, DC, 128])
                    p.add("vector", lambda e, qoff=qoff, ibc=ibc: e.tensor_tensor(
                        Obf[:, :, :], Oacc[:, :, qoff:qoff + 128], ibc,
                        op=ALU.mult),
                        waits=[("T", EPV.get("outproj", 0)), ("V", v_invl),
                               ("fF", f_all)],
                        inc="V")
                    v_obf = p.val("V")
                    ows = [("V", v_obf), ("V", EPV.get("attnsb", 0))]
                    for d in range(DC):
                        for hh in range(2):
                            p.add("tensor", lambda e, d=d, hh=hh: e.matmul(
                                peA[:, hh, :], Obf[:, d, :],
                                woutT[:, d, hh * 512:(hh + 1) * 512],
                                start=(d == 0), stop=(d == DC - 1)),
                                waits=(ows if (d == 0 and hh == 0) else []),
                                inc=("T" if (d == DC - 1 and hh == 1) else None))
                    EPV["outproj"] = p.val("T")

                def frontB(lt):
                    gt = T0 + lt
                    par = lt % 2
                    t_op = EPV["outproj"]
                    for hh in range(2):
                        p.add("vector", lambda e, hh=hh: e.tensor_tensor(
                            attn_sb[:, hh * 512:(hh + 1) * 512], peA[:, hh, :],
                            biasb[:, hh * 512:(hh + 1) * 512], op=ALU.add),
                            waits=([("T", t_op)] if hh == 0 else []),
                            inc=("V" if hh == 1 else None))
                    EPV["attnsb"] = p.val("V")
                    if dbg:
                        p.add("gpsimd", lambda e, gt=gt: e.dma_start(
                            DBGA[gt * 128:(gt + 1) * 128, :], attn_sb[:, :]),
                            waits=[("V", EPV["attnsb"])], inc="qOut", amt=16)
                        out_n[0] += 1
                    p.add("vector", lambda e: e.bn_stats(bnstE[:, 0:6], attn_sb[:, 0:512]))
                    p.add("vector", lambda e: e.bn_stats(bnstE[:, 6:12], attn_sb[:, 512:1024]), inc="V")
                    _bv = p.val("V")
                    p.add("vector", lambda e: e.bn_aggr(mvE[:, 0:2], bnstE[:, :]),
                          waits=[("V", _bv)], inc="V")
                    v_bn1 = p.val("V")
                    g_r1 = rstd_pow(0, v_bn1)
                    p.add("vector", lambda e: e.tensor_scalar(
                        ffn_sb[:, :], attn_sb[:, :], mvE[:, 0:1], rstdE[:, 1:2],
                        op0=ALU.subtract, op1=ALU.mult), waits=[("G", g_r1)])
                    p.add("vector", lambda e, par=par: e.tensor_tensor(
                        x1t[:, par, :], ffn_sb[:, :], x0t[:, par, :], op=ALU.add),
                        waits=[("qS", EPV[("qx0", lt)])], inc="V")
                    E_x1[lt] = p.val("V")
                    EPV["x1p"] = E_x1[lt]
                    if dbg:
                        p.add("gpsimd", lambda e, gt=gt, par=par: e.dma_start(
                            DBGX1[gt * 128:(gt + 1) * 128, :], x1t[:, par, :]),
                            waits=[("V", E_x1[lt])], inc="qOut", amt=16)
                        out_n[0] += 1
                    p.add("vector", lambda e, par=par: e.bn_stats(bnstE[:, 0:6], x1t[:, par, 0:512]))
                    p.add("vector", lambda e, par=par: e.bn_stats(bnstE[:, 6:12], x1t[:, par, 512:1024]), inc="V")
                    _bv = p.val("V")
                    p.add("vector", lambda e: e.bn_aggr(mvE[:, 2:4], bnstE[:, :]),
                          waits=[("V", _bv)], inc="V")
                    v_bn2 = p.val("V")
                    g_r2 = rstd_pow(1, v_bn2)
                    p.add("vector", lambda e, par=par: e.tensor_scalar(
                        h2t[:, :], x1t[:, par, :], mvE[:, 2:3], rstdE[:, 3:4],
                        op0=ALU.subtract, op1=ALU.mult),
                        waits=[("G", g_r2), ("T", E_th2.get(lt - 1, 0))], inc="V")
                    EPV["vh2"] = p.val("V")

                def frontC_T(lt):
                    tws = [("V", EPV["vh2"]), ("V", E_tev.get(lt - 2, 0))]
                    for c in range(DC):
                        p.add("tensor", lambda e, c=c, lt=lt: _tp(e,
                            peT[:, lt % 2, c, :], h2t[:, c * 128:(c + 1) * 128],
                            eyeb[:, :]), waits=(tws if c == 0 else []),
                            inc=("T" if c == DC - 1 else None))
                    E_th2[lt] = p.val("T")

                def frontC_V(lt):
                    p.add("vector", lambda e, lt=lt: e.tensor_copy(
                        h2f[:, :, 1:129], peT[:, lt % 2, :, :]),
                        waits=[("T", E_th2[lt])], inc="V")
                    E_tev[lt] = p.val("V")
                    h2tv = h2t.ap().rearrange("p (c k) -> p c k", k=128)
                    p.add("vector", lambda e, h2tv=h2tv: e.tensor_tensor(
                        h2tv[:, :, :], h2f[:, :, 1:129], h2f[:, :, 0:128],
                        op=ALU.subtract))
                    p.add("vector", lambda e, h2tv=h2tv: e.tensor_tensor(
                        xr[:, :, :], h2tv[:, :, :], mure[:, :, :], op=ALU.mult))
                    p.add("vector", lambda e: e.tensor_tensor(
                        xr[:, :, :], xr[:, :, :], h2f[:, :, 0:128], op=ALU.add))
                    p.add("vector", lambda e, h2tv=h2tv: e.tensor_tensor(
                        xk[:, :, :], h2tv[:, :, :], muke[:, :, :], op=ALU.mult))
                    p.add("vector", lambda e: e.tensor_tensor(
                        xk[:, :, :], xk[:, :, :], h2f[:, :, 0:128], op=ALU.add))
                    p.add("vector", lambda e: e.tensor_copy(
                        h2f[:, :, 0:1], h2f[:, :, 128:129]), inc="V")
                    EPV["xrk"] = p.val("V")

                def backT1(lt):
                    kws = [("V", EPV["xrk"]), ("S", EPV.get("sigmoid", 0))]
                    t_kf = []
                    for b in range(2):
                        for j in range(4):
                            dout = 4 * b + j
                            for c in range(DC):
                                p.add("tensor",
                                      lambda e, b=b, j=j, dout=dout, c=c: e.matmul(
                                          peK[:, b, j * 128:(j + 1) * 128],
                                          wkT[:, c, dout * 128:(dout + 1) * 128],
                                          xk[:, c, :],
                                          start=(j == 0 and c == 0), stop=(c == DC - 1),
                                          skip_group_check=True),
                                      waits=(kws if (b == 0 and j == 0 and c == 0)
                                             else []),
                                      inc=("T" if (j == 3 and c == DC - 1) else None))
                        t_kf.append(p.val("T"))
                    EPV["tkf"] = t_kf

                def backSV1(lt):
                    t_kf = EPV["tkf"]
                    for b in range(2):
                        p.add("scalar", lambda e, b=b: e.activation(
                            kr_sb[:, b, :], peK[:, b, :], AF.Relu),
                            waits=[("T", t_kf[b])], inc=("S" if b == 1 else None))
                    EPV["relu"] = p.val("S")
                    for b in range(2):
                        p.add("scalar", lambda e, b=b: e.activation(
                            k_act[:, b * 4:(b + 1) * 4, :], kr_sb[:, b, :],
                            AF.Square), inc=("S" if b == 1 else None))
                    EPV["ka"] = ("S", p.val("S"))

                def backT2(lt):
                    rws = [("V", EPV["xrk"]), ("S", EPV["relu"]), EPV["ka"]]
                    for c in range(DC):
                        for hh in range(2):
                            p.add("tensor", lambda e, c=c, hh=hh: e.matmul(
                                peK[:, hh, :], xr[:, c, :],
                                wrT[:, c, hh * 512:(hh + 1) * 512],
                                start=(c == 0), stop=(c == DC - 1)),
                                waits=(rws if (c == 0 and hh == 0) else []),
                                inc=("T" if (c == DC - 1 and hh == 1) else None))
                    t_r = p.val("T")
                    for hh in range(2):
                        p.add("scalar", lambda e, hh=hh: e.activation(
                            r_sb[:, hh * 512:(hh + 1) * 512], peK[:, hh, :],
                            AF.Sigmoid),
                            waits=([("T", t_r)] if hh == 0 else []),
                            inc=("S" if hh == 1 else None))
                    EPV["sigmoid"] = p.val("S")

                def backT3(lt):
                    vws = [EPV["ka"], ("V", EPV.get("rv", 0))]
                    for c in range(DC):
                        for hh in range(2):
                            p.add("tensor", lambda e, c=c, hh=hh: e.matmul(
                                peV[:, hh, :], k_act[:, c, :],
                                wvT[:, c, hh * 512:(hh + 1) * 512],
                                start=(c == 0), stop=(c == DC - 1)),
                                waits=(vws if (c == 0 and hh == 0) else []),
                                inc=("T" if (c == DC - 1 and hh == 1) else None))
                    EPV["tv"] = p.val("T")

                def backV(lt):
                    gt = T0 + lt
                    par = lt % 2
                    t_v = EPV["tv"]
                    s_sig = EPV["sigmoid"]
                    for hh in range(2):
                        p.add("vector", lambda e, hh=hh: e.tensor_tensor(
                            ffn_sb[:, hh * 512:(hh + 1) * 512],
                            r_sb[:, hh * 512:(hh + 1) * 512], peV[:, hh, :],
                            op=ALU.mult),
                            waits=([("T", t_v), ("S", s_sig)] if hh == 0 else []),
                            inc=("V" if hh == 1 else None))
                    EPV["rv"] = p.val("V")
                    p.add("vector", lambda e: e.bn_stats(bnstE[:, 0:6], ffn_sb[:, 0:512]))
                    p.add("vector", lambda e: e.bn_stats(bnstE[:, 6:12], ffn_sb[:, 512:1024]), inc="V")
                    _bv = p.val("V")
                    p.add("vector", lambda e: e.bn_aggr(mvE[:, 4:6], bnstE[:, :]),
                          waits=[("V", _bv)], inc="V")
                    v_bn3 = p.val("V")
                    g_r3 = rstd_pow(2, v_bn3)
                    p.add("vector", lambda e: e.tensor_scalar(
                        attn_sb[:, :], ffn_sb[:, :], mvE[:, 4:5], rstdE[:, 5:6],
                        op0=ALU.subtract, op1=ALU.mult), waits=[("G", g_r3)])
                    p.add("vector", lambda e, par=par: e.tensor_tensor(
                        x2sb[:, :], attn_sb[:, :], x1t[:, par, :], op=ALU.add),
                        waits=[("qOut", 16 * out_n[0])], inc="V")
                    v_x2 = p.val("V")
                    p.add("gpsimd", lambda e, gt=gt: e.dma_start(
                        Y[gt * 128:(gt + 1) * 128, :], x2sb[:, :]),
                        waits=[("V", v_x2)], inc="qOut", amt=16)
                    out_n[0] += 1

                if NTE > 0:
                    frontA(0)
                    frontB(0)
                    frontC_T(0)
                    frontC_V(0)
                for lt in range(NTE):
                    backT1(lt)
                    if lt + 1 < NTE:
                        frontA(lt + 1)
                    backSV1(lt)
                    if lt + 1 < NTE:
                        frontB(lt + 1)
                    backT2(lt)
                    backT3(lt)
                    if lt + 1 < NTE:
                        frontC_T(lt + 1)
                    backV(lt)
                    if lt + 1 < NTE:
                        frontC_V(lt + 1)

                p.emit_into(blk)
            blkg.__exit__(None, None, None)

        p.add("sync", lambda e: e.nop(), waits=[("qOut", 16 * out_n[0])])
        p.block()

    return nc


_NC = None


def kernel(L0_attn_x, L0_attn_weight_qkv, L0_attn_out_w, L0_attn_out_b,
           L0_ffn_W_r, L0_ffn_W_k, L0_ffn_W_v, L0_ffn_mu_r, L0_ffn_mu_k):
    global _NC
    x = np.asarray(L0_attn_x, dtype=np.float32)
    wqkv = np.ascontiguousarray(np.asarray(L0_attn_weight_qkv, dtype=np.float32))
    wout = np.ascontiguousarray(np.asarray(L0_attn_out_w, dtype=np.float32))
    outb = np.ascontiguousarray(np.asarray(L0_attn_out_b, dtype=np.float32))
    wr = np.ascontiguousarray(np.asarray(L0_ffn_W_r, dtype=np.float32))
    wk = np.ascontiguousarray(np.asarray(L0_ffn_W_k, dtype=np.float32))
    wv = np.ascontiguousarray(np.asarray(L0_ffn_W_v, dtype=np.float32))
    mur = np.ascontiguousarray(np.asarray(L0_ffn_mu_r, dtype=np.float32))
    muk = np.ascontiguousarray(np.asarray(L0_ffn_mu_k, dtype=np.float32))
    eye = np.eye(128, dtype=np.float32)

    if _NC is None:
        _NC = build()

    in_maps = []
    for c in range(8):
        b, h = c // 2, c % 2
        r0 = 0 if h == 0 else NSEQ - NQ
        in_maps.append({
            "XQ": np.ascontiguousarray(x[b, r0:r0 + NQ]),
            "XF": np.ascontiguousarray(x[b]),
            "WQKV": wqkv, "WOUT": wout, "OUTB": outb,
            "WR": wr, "WK": wk, "WV": wv, "MUR": mur, "MUK": muk, "EYE": eye,
        })
    res = run_bass_kernel_spmd(_NC, in_maps, core_ids=list(range(8)))
    out = np.empty((4, NSEQ, D), dtype=np.float32)
    for c in range(8):
        b, h = c // 2, c % 2
        y = np.asarray(res.results[c]["Y"])
        if h == 0:
            out[b, 0:NQ] = y
        else:
            out[b, NQ:NSEQ] = y[NQ - (NSEQ - NQ):NQ]
    return out
